# version 09 (final)
# baseline (speedup 1.0000x reference)
import numpy as np
from contextlib import ExitStack

import concourse.bass as bass
import concourse.mybir as mybir
from concourse.bass_utils import run_bass_kernel_spmd

F32 = mybir.dt.float32
BF16 = mybir.dt.bfloat16
I32 = mybir.dt.int32
AF = mybir.ActivationFunctionType
ALU = mybir.AluOpType
AX = mybir.AxisListType.X

D = 1024
NT = 4096
SEQ = 2048
TG = 512
NG = NT // TG
DH = 4096
NH = 16
HD = 64
CAP = 512
DFF = 2730
LN_EPS = 1e-5
RMS_EPS = 1e-6
NEG = -1e30

IC = D // 128
HC = DH // 128

ENGS = ("sync", "scalar", "vector", "tensor", "gpsimd")


class KB:

    def __init__(self, nc, ctx):
        self.nc = nc
        self.ctx = ctx
        self.ops = {e: [] for e in ENGS}
        self.sems = {}
        self.cnt = {}
        self.phase_id = 0
        self.cur = None
        self.prev_edges = []
        self._first_in_phase = {}
        self._last_op = {}

    def _mksem(self, name):
        h = self.ctx.enter_context(self.nc.semaphore(name))
        self.sems[name] = h
        self.cnt[name] = 0
        return name

    def begin_phase(self):
        self.phase_id += 1
        p = self.phase_id
        self.cur = {k: self._mksem(f"p{p}{k}") for k in
                    ("qS", "qA", "qP", "tM", "sE", "vE", "pE")}
        self._first_in_phase = {e: True for e in ENGS}
        self._last_op = {e: None for e in ENGS}

    def end_phase(self):
        self.prev_edges = [(s, self.cnt[s]) for s in self.cur.values()
                           if self.cnt[s] > 0]

    def _emit(self, eng, fn, wait, inc, is_dma):
        w = [t for t in wait if t is not None]
        if self._first_in_phase.get(eng):
            self._first_in_phase[eng] = False
            w = list(self.prev_edges) + w
        ic = []
        for s, amt in inc:
            self.cnt[s] += amt
            ic.append((s, amt))
        rec = [fn, w, ic, is_dma]
        self.ops[eng].append(rec)
        self._last_op[eng] = rec
        return rec

    def dma(self, queue, fn, wait=()):
        qsem = {"sync": "qS", "scalar": "qA", "gpsimd": "qP"}[queue]
        s = self.cur[qsem]
        self._emit(queue, fn, wait, [(s, 16)], True)
        return (s, self.cnt[s])

    def op(self, eng, fn, wait=()):
        sem = {"scalar": "sE", "vector": "vE", "tensor": "tM", "gpsimd": "pE"}[eng]
        s = self.cur[sem]
        self._emit(eng, fn, wait, [(s, 1)], False)
        return (s, self.cnt[s])

    def mms(self, fns, wait=()):
        s = self.cur["tM"]
        for i, fn in enumerate(fns):
            self._emit("tensor", fn, wait if i == 0 else (),
                       [(s, 1)] if i == len(fns) - 1 else [], False)
        return (s, self.cnt[s])

    def render(self):
        nc = self.nc
        with nc.Block() as block:
            for eng in ENGS:
                if not self.ops[eng]:
                    continue

                def make_body(eng_ops):
                    def body(E):
                        for fn, w, ic, _ in eng_ops:
                            for s, v in w:
                                if v > 0:
                                    E.wait_ge(self.sems[s], v)
                            inst = fn(E)
                            for s, amt in ic:
                                inst.then_inc(self.sems[s], amt)
                    return body
                getattr(block, eng)(make_body(self.ops[eng]))



def _t(a):
    return np.ascontiguousarray(np.asarray(a, np.float32).T)


W_SHAPES = {
    "w1T": (D, DH), "w2T": (D, DH), "h1T": (DH, DH), "h2T": (DH, DH),
    "w3T": (DH, D),
    "Wq0T": (D, D), "Wk0T": (D, D), "Wv0T": (D, D),
    "Wq1": (D, D), "Wk1": (D, D), "Wv1": (D, D), "Wo1": (D, D),
    "WqkvT": (D, 3 * D), "WoutT": (D, D),
    "Wf1T": (D, DFF), "Wf2T": (D, DFF), "Wf3T": (DFF, D),
}

NCOL = 220
NROW = 5
NMSK = 896 + 256 + 128


def prep_shared(inp):
    sh = {}
    sh["w1T"] = _t(inp["L0_ffn_w1_w"])
    sh["w2T"] = _t(inp["L0_ffn_w2_w"])
    sh["h1T"] = _t(inp["L0_ffn_h1_w"])
    sh["h2T"] = _t(inp["L0_ffn_h2_w"])
    sh["w3T"] = _t(inp["L0_ffn_w3_w"])
    sh["Wq0T"] = _t(inp["L0_attn_Wq"])
    sh["Wk0T"] = _t(inp["L0_attn_Wk"])
    sh["Wv0T"] = _t(inp["L0_attn_Wv"])
    sh["Wq1"] = np.ascontiguousarray(np.asarray(inp["L1_attn_Wq"], np.float32))
    sh["Wk1"] = np.ascontiguousarray(np.asarray(inp["L1_attn_Wk"], np.float32))
    sh["Wv1"] = np.ascontiguousarray(np.asarray(inp["L1_attn_Wv"], np.float32))
    sh["Wo1"] = np.ascontiguousarray(np.asarray(inp["L1_attn_Wo"], np.float32))
    sh["WqkvT"] = _t(inp["L1_ffn_W_qkv"])
    sh["WoutT"] = _t(inp["L1_ffn_W_out"])
    sh["Wf1T"] = _t(inp["L1_ffn_W1"])
    sh["Wf2T"] = _t(inp["L1_ffn_W2"])
    sh["Wf3T"] = _t(inp["L1_ffn_W3"])

    cols = []
    colidx = {}

    def addcols(name, arr):
        arr = np.asarray(arr, np.float32).reshape(-1)
        colidx[name] = len(cols)
        for c in range(arr.size // 128):
            cols.append(arr[c * 128:(c + 1) * 128])

    addcols("ones", np.ones(128, np.float32))
    addcols("eps_ln", np.full(128, LN_EPS, np.float32))
    addcols("eps_rms", np.full(128, RMS_EPS, np.float32))
    addcols("s1", inp["L0_ffn_w1_s"])
    addcols("s2", inp["L0_ffn_w2_s"])
    addcols("sh1", inp["L0_ffn_h1_s"])
    addcols("sh2", inp["L0_ffn_h2_s"])
    addcols("s3", inp["L0_ffn_w3_s"])
    addcols("gamma", inp["L0_attn_gamma"])
    addcols("beta", inp["L0_attn_beta"])
    addcols("bq0", inp["L0_attn_bq"])
    addcols("bk0", inp["L0_attn_bk"])
    addcols("bq1", inp["L1_attn_bq"])
    addcols("bk1", inp["L1_attn_bk"])
    addcols("bo1", inp["L1_attn_bo"])
    addcols("wr", inp["L1_ffn_w_router"])
    addcols("iota", np.arange(2048, dtype=np.float32))
    cp = np.zeros((128, NCOL), np.float32)
    for i, c in enumerate(cols):
        cp[:, i] = c
    sh["colpack"] = cp
    sh["colidx"] = colidx

    rowpack = np.zeros((65, 2176), np.float32)
    rowidx = {}

    def addrow(name, p0, c0, arr):
        arr = np.asarray(arr, np.float32).reshape(-1)
        rowidx[name] = (p0, c0)
        rowpack[p0, c0:c0 + arr.size] = arr

    addrow("bv0", 0, 0, inp["L0_attn_bv"])
    addrow("bv1", 32, 0, inp["L1_attn_bv"])
    addrow("g1", 64, 0, inp["L1_ffn_g1"])
    addrow("g2", 0, 1024, inp["L1_ffn_g2"])
    addrow("ones32", 32, 1024, np.ones(128, np.float32))
    addrow("ones64", 64, 1024, np.ones(128, np.float32))
    addrow("ones", 0, 2048, np.ones(128, np.float32))
    sh["rowpack"] = rowpack
    sh["rowidx"] = rowidx

    p = np.arange(128)[:, None]
    masks = []
    maskidx = {}

    def addmask(name, m):
        maskidx[name] = sum(x.shape[1] for x in masks)
        masks.append(np.asarray(m, np.float32))

    v896 = np.arange(896)[None, :]
    addmask("l0base", (v896 >= (p + 384)).astype(np.float32))
    j = np.arange(128)[None, :]
    band = np.concatenate([(j >= p).astype(np.float32),
                           (j <= p).astype(np.float32)], axis=1)
    addmask("l1band", band)
    addmask("ltri", (j <= p).astype(np.float32))
    import ml_dtypes
    sh["maskpack"] = np.ascontiguousarray(
        np.concatenate(masks, axis=1).astype(ml_dtypes.bfloat16))
    sh["maskidx"] = maskidx

    sh["iotab"] = np.ascontiguousarray(
        np.broadcast_to(np.arange(512, dtype=np.float32)[None, :], (128, 512)))
    sh["idf32"] = np.eye(128, dtype=np.float32)
    import ml_dtypes
    sh["idbf"] = np.eye(128, dtype=ml_dtypes.bfloat16)
    return sh


def prep_core(inp, core):
    emb = np.asarray(inp["L0_attn_embedded"], np.float32)
    sl = emb[2 * core:2 * core + 2]
    return {"x_in": np.ascontiguousarray(sl.transpose(2, 0, 1).reshape(D, NT))}



class Prog:
    def __init__(self):
        self.ctx = ExitStack()
        nc = bass.Bass("TRN2", target_bir_lowering=False)
        self.nc = nc
        e = self.ctx.enter_context
        dt = nc.dram_tensor

        self.x_in = dt("x_in", [D, NT], F32, kind="ExternalInput").ap()
        self.colpack_d = dt("colpack", [128, NCOL], F32, kind="ExternalInput").ap()
        self.rowpack_d = dt("rowpack", [65, 2176], F32, kind="ExternalInput").ap()
        self.maskpack_d = dt("maskpack", [128, NMSK], BF16, kind="ExternalInput").ap()
        self.idf32_d = dt("idf32", [128, 128], F32, kind="ExternalInput").ap()
        self.idbf_d = dt("idbf", [128, 128], BF16, kind="ExternalInput").ap()
        self.win = {}
        self.wc = {}
        for name, shape in W_SHAPES.items():
            self.win[name] = dt(name, list(shape), F32, kind="ExternalInput").ap()
            self.wc[name] = dt("c" + name, list(shape), BF16, kind="Internal").ap()

        self.out_d = dt("out", [D, NT], F32, kind="ExternalOutput").ap()
        self.x_res = dt("x_res", [D, NT], F32, kind="Internal").ap()
        self.xn_tm = dt("xn_tm", [NT, D], F32, kind="Internal").ap()
        self.corr0_d = dt("corr0", [NT, D], F32, kind="ExternalInput").ap()
        self.corr1_d = dt("corr1", [NT, D], F32, kind="ExternalInput").ap()
        self.xn_fm = dt("xn_fm", [D, NT], F32, kind="Internal").ap()
        self.z_scr = dt("z_scr", [D, SEQ], F32, kind="Internal").ap()
        self.corr_d = dt("corr", [NT, D], F32, kind="Internal").ap()

        sb = lambda n, s, d: e(nc.sbuf_tensor(n, s, d))
        self.BIG0 = sb("BIG0", [128, 16384], BF16)
        self.BIG1 = sb("BIG1", [128, 16384], BF16)
        self.BIG2 = sb("BIG2", [128, 16384], BF16)
        self.F = sb("F", [128, 8192], F32)
        self.XG = sb("XG", [128, 4096], F32)
        self.W0 = sb("W0", [128, 4096], BF16)
        self.W1 = sb("W1", [128, 4096], BF16)
        self.LNS = sb("LNS", [128, 8192], BF16)
        self.ASL = sb("ASL", [128, 512], BF16)
        self.COL = sb("COL", [128, NCOL], F32)
        self.COLB = sb("COLB", [128, 2], BF16)
        self.S3H = sb("S3H", [128, IC], F32)
        self.ROWP = sb("ROWP", [65, 2176], F32)
        self.MSK = sb("MSK", [128, NMSK], BF16)
        self.IDF = sb("IDF", [128, 128], F32)
        self.IDB = sb("IDB", [128, 128], BF16)
        self.SCR = sb("SCR", [1, 2048], F32)
        self.SROW = sb("SROW", [1, 2048], F32)
        self.SCOL = sb("SCOL", [128, 16], F32)
        self.RNK = sb("RNK", [128, 16], F32)
        self.IDXC = sb("IDXC", [128, 4], I32)
        self.SMALL = sb("SMALL", [128, 16], F32)

        self.PS = e(nc.psum_tensor("PS", [128, 3584], F32))
        self.PT = e(nc.psum_tensor("PT", [128, 1024], BF16))

        self.k = KB(nc, self.ctx)
        self.bko = {b: None for b in range(8)}
        self.colidx = None
        self.rowidx = None
        self.maskidx = None
        self.ln_prev = {}
        self.xg_free = None
        self.xn_free = None

    def bank(self, b, parts=128, width=512):
        return self.PS[0:parts, b * 512:b * 512 + width]

    def bw(self, *banks):
        return [self.bko[b] for b in banks if self.bko[b] is not None]

    def bset(self, b, t):
        self.bko[b] = t

    def col(self, name, i=0):
        j = self.colidx[name] + i
        return self.COL[:, j:j + 1]

    def row(self, name, lo=0, hi=1024):
        p0, c0 = self.rowidx[name]
        return self.ROWP[p0:p0 + 1, c0 + lo:c0 + hi]

    def ones_for(self, name):
        p0 = self.rowidx[name][0]
        return self.row({0: "ones", 32: "ones32", 64: "ones64"}[p0], 0, 128)

    def mask(self, name, w):
        m = self.maskidx[name]
        return self.MSK[:, m:m + w]

    def l0diag(self, mvar):
        m = self.maskidx["l0base"] + (3 - mvar) * 128
        return self.MSK[:, m:m + 512]

    @staticmethod
    def rep(ap, n):
        return bass.AP(ap.tensor, ap.offset, [ap.ap[0], [0, n], ap.ap[1]])


    def phase_init(self):
        k = self.k
        k.begin_phase()
        t1 = k.dma("sync", lambda E: E.dma_start(self.COL[:, :], self.colpack_d))
        k.dma("sync", lambda E: E.dma_start(self.ROWP[:, :], self.rowpack_d))
        k.dma("sync", lambda E: E.dma_start(self.MSK[:, :], self.maskpack_d))
        k.dma("sync", lambda E: E.dma_start(self.IDF[:, :], self.idf32_d))
        tl = k.dma("sync", lambda E: E.dma_start(self.IDB[:, :], self.idbf_d))
        k.op("vector", lambda E: E.tensor_copy(self.COLB[:, 0:1], self.col("ones")),
             wait=[t1, tl])
        ci = self.colidx["s3"]
        k.op("vector", lambda E: E.tensor_scalar_mul(
            self.S3H[:, :], self.COL[:, ci:ci + IC], 0.5), wait=[t1])
        k.end_phase()

    def phase_wconv(self, names=None):
        k = self.k
        k.begin_phase()
        tiles = []
        for name in (names or list(self.win)):
            src = self.win[name]
            dst = self.wc[name]
            R, C = W_SHAPES[name]
            r = 0
            while r < R:
                rr = min(128, R - r)
                tiles.append((src, dst, r, rr, C))
                r += 128
        half = [self.F[:, 0:4096], self.F[:, 4096:8192]]
        halfb = [self.W0[:, 0:4096], self.W1[:, 0:4096]]
        last_cast = [None, None]
        last_store = [None, None]
        for i, (src, dst, r, rr, C) in enumerate(tiles):
            h = i % 2
            fin = half[h][0:rr, 0:C]
            fout = halfb[h][0:rr, 0:C]
            tl = k.dma("sync", lambda E, a=fin, b=src[r:r + rr, :]: E.dma_start(a, b),
                       wait=[last_cast[h]])
            tc = k.op("vector", lambda E, a=fout, b=fin: E.tensor_copy(a, b),
                      wait=[tl, last_store[h]])
            last_cast[h] = tc
            ts = k.dma("gpsimd",
                       lambda E, a=dst[r:r + rr, :], b=fout: E.dma_start(a, b),
                       wait=[tc])
            last_store[h] = ts
        k.end_phase()

    def ln_group(self, src, t0, exact=False, eps_name="eps_ln",
                 want_fp32=False, xe_buf=None, sqe_buf=None,
                 x_sbuf=None, x_sbuf_t=None, epilogue=True):
        k = self.k
        xg = self.XG if x_sbuf is None else x_sbuf
        xn_bf = self.LNS[:, 0:4096]
        xbf = self.LNS[:, 4096:8192]
        TMP = self.F[:, 0:4096]
        S = TG
        pv = self.ln_prev

        if x_sbuf is None:
            tload = k.dma("sync", lambda E: E.dma_start(
                xg[:, :].rearrange("p (c t) -> p c t", c=IC),
                src[:, t0:t0 + S].rearrange("(c p) t -> p c t", p=128)),
                wait=[self.xg_free, pv.get("xg_read")])
        else:
            tload = x_sbuf_t
        tcast = k.op("vector", lambda E: E.tensor_copy(xbf[:, :], xg[:, :]),
                     wait=[tload, pv.get("tssq")])

        onesb = self.COLB[:, 0:1]
        psum_s = self.PS[0:1, 4 * 512:4 * 512 + 512]
        psum_q = self.PS[0:1, 5 * 512:5 * 512 + 512]

        def sum_fns(psum, srcs, n_tot):
            fns = []
            i = 0
            for srct in srcs:
                for c in range(IC):
                    fns.append(lambda E, a=psum, b=onesb,
                               r=srct[:, c * S:(c + 1) * S],
                               st=(i == 0), sp=(i == n_tot - 1):
                               E.matmul(a, b, r, start=st, stop=sp))
                    i += 1
            return fns

        if not exact:
            tsum = k.mms(sum_fns(psum_s, [xbf], IC),
                         wait=[tcast] + self.bw(4))
            tsq = k.op("scalar", lambda E: E.activation(xbf[:, :], xbf[:, :],
                                                        AF.Square), wait=[tsum])
            tssq = k.mms(sum_fns(psum_q, [xbf], IC),
                         wait=[tsq] + self.bw(5))
        else:
            xe = xe_buf
            sqe = sqe_buf
            sqf = self.F[:, 4096:8192]
            txe = k.op("vector", lambda E: E.tensor_sub(xe[:, :], xg[:, :],
                                                        xbf[:, :]),
                       wait=[tcast, pv.get("txe_free")])
            tsqf = k.op("vector", lambda E: E.tensor_mul(sqf[:, :], xg[:, :],
                                                         xg[:, :]),
                        wait=[tload, pv.get("sqf_free")])
            tsum = k.mms(sum_fns(psum_s, [xbf, xe], 2 * IC),
                         wait=[tcast, txe] + self.bw(4))
            tsqb = k.op("vector", lambda E: E.tensor_copy(xbf[:, :], sqf[:, :]),
                        wait=[tsqf, tsum])
            tsqe = k.op("vector", lambda E: E.tensor_sub(sqe[:, :], sqf[:, :],
                                                         xbf[:, :]),
                        wait=[tsqb, pv.get("sqe_free")])
            tssq = k.mms(sum_fns(psum_q, [xbf, sqe], 2 * IC),
                         wait=[tsqb, tsqe] + self.bw(5))
            tsq = tssq

        SC = self.SCR
        mu = SC[0:1, 0:512]
        msd = SC[0:1, 512:1024]
        mrr = SC[0:1, 1024:1536]
        dvu = SC[0:1, 1536:2048]
        etq = psum_s
        eps_val = LN_EPS if eps_name == "eps_ln" else RMS_EPS

        t1 = k.op("scalar", lambda E: E.activation(mu, psum_s, AF.Copy,
                                                   scale=1.0 / D),
                  wait=[tsum, pv.get("t11"), pv.get("tb2")])
        self.bset(4, t1)
        t2 = k.op("scalar", lambda E: E.activation(msd, psum_q, AF.Copy,
                                                   scale=1.0 / D),
                  wait=[tssq, pv.get("tb1"), pv.get("t11")])
        t3 = k.op("scalar", lambda E: E.activation(mrr, mu, AF.Square),
                  wait=[t1, pv.get("t10")])
        t4a = k.op("vector", lambda E: E.tensor_sub(dvu, msd, mrr),
                   wait=[t2, t3, pv.get("t5")])
        t4 = k.op("vector", lambda E: E.tensor_scalar_add(dvu, dvu, eps_val),
                  wait=[t4a])
        t5 = k.op("scalar", lambda E: E.activation(msd, dvu, AF.Sqrt),
                  wait=[t4])
        t6 = k.op("vector", lambda E: E.reciprocal(mrr, msd), wait=[t5])
        t7 = k.op("vector", lambda E: E.tensor_mul(etq, dvu, mrr),
                  wait=[t6] + self.bw(4))
        t8 = k.op("vector", lambda E: E.tensor_add(dvu, msd, etq), wait=[t7])
        self.bset(4, t8)
        t9 = k.op("vector", lambda E: E.reciprocal(mrr, dvu), wait=[t8])
        t10 = k.op("vector", lambda E: E.tensor_scalar_mul(msd, mrr, 2.0),
                   wait=[t9])
        t11 = k.op("vector", lambda E: E.tensor_mul(mu, mu, msd), wait=[t10])
        rstd = msd
        mur = mu

        onesrow = self.row("ones", 0, 128)
        bc_r_ps = self.bank(5)
        bc_m_ps = self.bank(6)
        tb1 = k.mms([lambda E: E.matmul(bc_r_ps, onesrow, rstd,
                                        start=True, stop=True)],
                    wait=[t10, t2] + self.bw(5))
        tb2 = k.mms([lambda E: E.matmul(bc_m_ps, onesrow, mur,
                                        start=True, stop=True)],
                    wait=[t11] + self.bw(6))
        self.ln_prev = {"tssq": tssq, "t5": t5, "t10": t10, "t11": t11,
                        "tb1": tb1, "tb2": tb2}
        if not epilogue:
            return {"xg": xg, "xg_t": tload, "bc_r_ps": bc_r_ps,
                    "bc_m_ps": bc_m_ps, "tb1": tb1, "tb2": tb2}
        r3 = lambda ap: ap.rearrange("p (c t) -> p c t", c=IC)
        tn1 = k.op("vector", lambda E: E.tensor_mul(
            r3(TMP[:, :]), r3(xg[:, :]), self.rep(bc_r_ps, IC)),
            wait=[tb1, pv.get("tmp_free")])
        self.bset(5, tn1)
        out_ap = TMP if want_fp32 else xn_bf
        tn2 = k.op("vector", lambda E: E.tensor_sub(
            r3(out_ap[:, :]), r3(TMP[:, :]), self.rep(bc_m_ps, IC)),
            wait=[tb2, tn1, None if want_fp32 else self.xn_free])
        self.bset(6, tn2)
        self.ln_prev["xg_read"] = tn1
        res = {"xg": xg, "xg_t": tload, "xn_bf": xn_bf, "rstd": rstd,
               "mur": mur, "TMP": TMP, "tsum": tsum, "tssq": tssq}
        if want_fp32:
            tn3 = k.op("vector", lambda E: E.tensor_copy(xn_bf[:, :], TMP[:, :]),
                       wait=[tn2, self.xn_free])
            res["xn_f32"] = TMP
            res["xn_f32_t"] = tn2
            res["xn_t"] = tn3
        else:
            res["xn_t"] = tn2
        return res

    def phase_l0ffn(self, src, dst):
        k = self.k
        k.begin_phase()
        self.ln_prev = {}
        self.xg_free = None
        self.xn_free = None
        self.bko = {b: None for b in range(8)}

        h_buf = self.BIG0
        g_buf = self.BIG1
        asil = self.ASL
        wbuf = [self.W0, self.W1]
        wfree = [None, None]

        for g in range(NG):
            t0 = g * TG
            ln = self.ln_group(src, t0)
            xn = ln["xn_bf"]
            xn_t = ln["xn_t"]
            xg = ln["xg"]

            last_te2 = None
            stage1_last_mm = None
            for gr in range(HC // 2):
                bi = gr % 2
                k.dma("sync", lambda E, b=wbuf[bi], gr=gr: E.dma_start(
                    b[:, 0:2048].rearrange("p (c o) -> p c o", c=IC),
                    self.wc["w1T"][:, gr * 256:(gr + 1) * 256]
                    .rearrange("(c p) o -> p c o", p=128)), wait=[wfree[bi]])
                dd = k.dma("sync", lambda E, b=wbuf[bi], gr=gr: E.dma_start(
                    b[:, 2048:4096].rearrange("p (c o) -> p c o", c=IC),
                    self.wc["w2T"][:, gr * 256:(gr + 1) * 256]
                    .rearrange("(c p) o -> p c o", p=128)))
                for j in range(2):
                    o = gr * 2 + j
                    ba = j * 2
                    bb = ba + 1
                    fa, fb = [], []
                    for c in range(IC):
                        lhs1 = wbuf[bi][:, c * 256 + j * 128: c * 256 + (j + 1) * 128]
                        lhs2 = wbuf[bi][:, 2048 + c * 256 + j * 128:
                                        2048 + c * 256 + (j + 1) * 128]
                        rhs = xn[:, c * TG:(c + 1) * TG]
                        fa.append(lambda E, a=self.bank(ba), l=lhs1, r=rhs,
                                  st=(c == 0), sp=(c == IC - 1):
                                  E.matmul(a, l, r, start=st, stop=sp))
                        fb.append(lambda E, a=self.bank(bb), l=lhs2, r=rhs,
                                  st=(c == 0), sp=(c == IC - 1):
                                  E.matmul(a, l, r, start=st, stop=sp))
                    ta = k.mms(fa, wait=[dd, xn_t] + self.bw(ba, bb))
                    tb = k.mms(fb)
                    stage1_last_mm = tb
                    te1 = k.op("scalar", lambda E, o=o, ba=ba: E.activation(
                        asil[:, :], self.bank(ba), AF.Silu,
                        scale=self.col("s1", o)), wait=[ta, last_te2])
                    te2 = k.op("vector", lambda E, o=o, bb=bb:
                               E.scalar_tensor_tensor(
                        h_buf[:, o * TG:(o + 1) * TG], self.bank(bb),
                        self.col("s2", o), asil[:, :], ALU.mult, ALU.mult),
                        wait=[tb, te1])
                    self.bset(ba, te1)
                    self.bset(bb, te2)
                    last_te2 = te2
                wfree[bi] = last_te2
            h_done = last_te2
            self.xn_free = stage1_last_mm

            def big_stage(src_buf, dst_buf, wname, scol, src_done):
                last = None
                for o in range(HC):
                    bi = o % 2
                    dd = k.dma("sync", lambda E, b=wbuf[bi], o=o, wn=wname:
                               E.dma_start(
                        b[:, :].rearrange("p (c o) -> p c o", c=HC),
                        self.wc[wn][:, o * 128:(o + 1) * 128]
                        .rearrange("(c p) o -> p c o", p=128)), wait=[wfree[bi]])
                    bk = o % 4
                    fs = []
                    for c in range(HC):
                        lhs = wbuf[bi][:, c * 128:(c + 1) * 128]
                        rhs = src_buf[:, c * TG:(c + 1) * TG]
                        fs.append(lambda E, a=self.bank(bk), l=lhs, r=rhs,
                                  st=(c == 0), sp=(c == HC - 1):
                                  E.matmul(a, l, r, start=st, stop=sp))
                    tm = k.mms(fs, wait=[dd, src_done] + self.bw(bk))
                    te = k.op("scalar", lambda E, o=o, bk=bk: E.activation(
                        dst_buf[:, o * TG:(o + 1) * TG], self.bank(bk),
                        AF.Gelu, scale=self.col(scol, o)), wait=[tm])
                    self.bset(bk, te)
                    wfree[bi] = tm
                    last = te
                return last

            g1_done = big_stage(h_buf, g_buf, "h1T", "sh1", h_done)
            g2_done = big_stage(g_buf, h_buf, "h2T", "sh2", g1_done)

            xg_last = None
            for o in range(IC):
                bi = o % 2
                dd = k.dma("sync", lambda E, b=wbuf[bi], o=o: E.dma_start(
                    b[:, :].rearrange("p (c o) -> p c o", c=HC),
                    self.wc["w3T"][:, o * 128:(o + 1) * 128]
                    .rearrange("(c p) o -> p c o", p=128)), wait=[wfree[bi]])
                bk = o % 4
                fs = []
                for c in range(HC):
                    lhs = wbuf[bi][:, c * 128:(c + 1) * 128]
                    rhs = h_buf[:, c * TG:(c + 1) * TG]
                    fs.append(lambda E, a=self.bank(bk), l=lhs, r=rhs,
                              st=(c == 0), sp=(c == HC - 1):
                              E.matmul(a, l, r, start=st, stop=sp))
                tm = k.mms(fs, wait=[dd, g2_done] + self.bw(bk))
                te = k.op("vector", lambda E, o=o, bk=bk:
                          E.scalar_tensor_tensor(
                    xg[:, o * TG:(o + 1) * TG], self.bank(bk),
                    self.S3H[:, o:o + 1], xg[:, o * TG:(o + 1) * TG],
                    ALU.mult, ALU.add), wait=[tm, ln["xg_t"]])
                self.bset(bk, te)
                wfree[bi] = tm
                xg_last = te

            self.xg_free = k.dma("gpsimd", lambda E, t0=t0: E.dma_start(
                dst[:, t0:t0 + TG].rearrange("(c p) t -> p c t", p=128),
                xg[:, :].rearrange("p (c t) -> p c t", c=IC)), wait=[xg_last])
        k.end_phase()
        return self.xg_free

    def phase_l0attn(self, src, dst, debug_z=None):
        k = self.k
        k.begin_phase()
        self.ln_prev = {}
        self.xg_free = None
        self.xn_free = None
        self.bko = {b: None for b in range(8)}

        qbuf, kbuf, vbuf = self.BIG0, self.BIG1, self.BIG2
        wbuf = [self.W0, self.W1]
        p_sb = self.W0[:, 0:2048]
        pT_sb = self.W0[:, 2048:2560]
        zf = self.F[:, 4096:8192]
        SM = self.SMALL
        ISQ = 1.0 / float(np.sqrt(SEQ))

        wfree = [None, None]
        for s in range(2):
            B = s * SEQ
            proj_last = None
            xn_wb = None
            for g4 in range(4):
                t0 = B + g4 * TG
                toff = g4 * TG
                ln = self.ln_group(src, t0, want_fp32=True)
                xn_bf = ln["xn_bf"]
                xn_t = ln["xn_t"]
                xn_wb = k.dma("gpsimd", lambda E, t0=t0: E.dma_start(
                    self.xn_fm[:, t0:t0 + TG].rearrange("(c p) t -> p c t", p=128),
                    ln["TMP"][:, :].rearrange("p (c t) -> p c t", c=IC)),
                    wait=[ln["xn_f32_t"]])
                self.ln_prev["tmp_free"] = xn_wb

                for wname, dbuf, bcol in (("Wq0T", qbuf, "bq0"), ("Wk0T", kbuf, "bk0")):
                    for og in range(2):
                        bi = og % 2
                        dd = k.dma("sync", lambda E, b=wbuf[bi], og=og, wn=wname:
                                   E.dma_start(
                            b[:, :].rearrange("p (c o) -> p c o", c=IC),
                            self.wc[wn][:, og * 512:(og + 1) * 512]
                            .rearrange("(c p) o -> p c o", p=128)),
                            wait=[wfree[bi]])
                        for j in range(4):
                            oc = og * 4 + j
                            bk = oc % 4
                            fs = []
                            for c in range(IC):
                                lhs = wbuf[bi][:, c * 512 + j * 128:
                                               c * 512 + (j + 1) * 128]
                                rhs = xn_bf[:, c * TG:(c + 1) * TG]
                                fs.append(lambda E, a=self.bank(bk), l=lhs, r=rhs,
                                          st=(c == 0), sp=(c == IC - 1):
                                          E.matmul(a, l, r, start=st, stop=sp))
                            tm = k.mms(fs, wait=[dd, xn_t] + self.bw(bk))
                            te = k.op("vector", lambda E, oc=oc, bk=bk, dbuf=dbuf,
                                      bcol=bcol, toff=toff: E.tensor_scalar(
                                dbuf[:, oc * SEQ + toff: oc * SEQ + toff + TG],
                                self.bank(bk), self.col(bcol, oc), None, ALU.add),
                                wait=[tm])
                            self.bset(bk, te)
                            wfree[bi] = tm
                            proj_last = te
                for og in range(2):
                    bi = og % 2
                    dd = k.dma("sync", lambda E, b=wbuf[bi], og=og: E.dma_start(
                        b[:, :].rearrange("p (c o) -> p c o", c=IC),
                        self.wc["Wv0T"][:, og * 512:(og + 1) * 512]
                        .rearrange("(c p) o -> p c o", p=128)), wait=[wfree[bi]])
                    for tt in range(4):
                        bk = tt % 4
                        fs = []
                        for c in range(IC):
                            lhs = xn_bf[:, c * TG + tt * 128: c * TG + (tt + 1) * 128]
                            rhs = wbuf[bi][:, c * TG:(c + 1) * TG]
                            fs.append(lambda E, a=self.bank(bk), l=lhs, r=rhs,
                                      st=(c == 0), sp=False:
                                      E.matmul(a, l, r, start=st, stop=sp))
                        fs.append(lambda E, a=self.bank(bk), og=og:
                                  E.matmul(a, self.row("ones", 0, 128),
                                           self.row("bv0", og * 512, (og + 1) * 512),
                                           start=False, stop=True))
                        tm = k.mms(fs, wait=[dd, xn_t] + self.bw(bk))
                        tc128 = g4 * 4 + tt
                        te = k.op("vector", lambda E, tc=tc128, og=og, bk=bk:
                                  E.tensor_copy(
                            vbuf[:, tc * 1024 + og * 512: tc * 1024 + (og + 1) * 512],
                            self.bank(bk)), wait=[tm])
                        self.bset(bk, te)
                        wfree[bi] = tm
                        proj_last = te
                self.xn_free = wfree[1]

            p_free = wfree[0]
            if s == 0:
                smrd = {0: None, 8: None}
                sb_ti = 0
            ZST = [self.F[0:64, 4096:4608], self.F[0:64, 4608:5120]]
            zst_free = [None, None]
            last_zd = None

            def emit_tp(pend):
                tmasks, qt, hb, h, first_of_tile = pend
                kbs = list(range(qt, 16))
                first_pv = True
                pv_last = None
                ttr = None
                qtl = qt % 4
                for pk in range(0, len(kbs), 4):
                    pack = kbs[pk:pk + 4]
                    ftr = []
                    for idx, kb in enumerate(pack):
                        ftr.append(lambda E, i=idx, kb=kb: E.transpose(
                            self.PT[:, i * 128:(i + 1) * 128],
                            p_sb[:, kb * 128:(kb + 1) * 128], self.IDB[:, :]))
                    ttr = k.mms(ftr, wait=tmasks + self.bw(7))
                    pevac = k.op("vector", lambda E, n=len(pack):
                                 E.tensor_copy(pT_sb[:, 0:n * 128],
                                               self.PT[:, 0:n * 128]),
                                 wait=[ttr])
                    self.bset(7, pevac)
                    fpv = []
                    for idx, kb in enumerate(pack):
                        last = (pk + idx == len(kbs) - 1)
                        fpv.append(lambda E, i=idx, kb=kb, st0=first_pv,
                                   sp=last, hb=hb, qtl=qtl, h=h: E.matmul(
                            self.PS[hb:hb + 64,
                                    4 * 512 + qtl * 128:
                                    4 * 512 + (qtl + 1) * 128],
                            vbuf[:, kb * 1024 + h * 64: kb * 1024 + h * 64 + 64],
                            pT_sb[:, i * 128:(i + 1) * 128],
                            start=st0 and i == 0, stop=sp,
                            tile_position=(0, hb) if hb else None))
                        if idx == 0:
                            first_pv = False
                    wpv = [pevac]
                    if pk == 0 and first_of_tile:
                        wpv += self.bw(4)
                    pv_last = k.mms(fpv, wait=wpv)
                return pv_last, ttr

            def emit_zout(h, st, pv_last):
                zi = (h * 4 + st) % 2
                zview = self.PS[hb_of(h):hb_of(h) + 64,
                                4 * 512:4 * 512 + 512].rearrange(
                    "p (m r) -> p r m", r=16)
                tzv = k.op("vector", lambda E, zi=zi, zv=zview: E.tensor_copy(
                    ZST[zi][:, :], zv), wait=[pv_last, zst_free[zi]])
                self.bset(4, tzv)
                doff = h * 128 + st * 32
                zdst = bass.AP(self.z_scr.tensor, doff,
                               [[SEQ, 64], [64 * SEQ, 16], [1, 32]])
                tzd = k.dma("gpsimd", lambda E, zi=zi, zd=zdst: E.dma_start(
                    zd, ZST[zi][:, :].rearrange("p (r m) -> p r m", r=16)),
                    wait=[tzv])
                zst_free[zi] = tzd
                return tzd

            def hb_of(h):
                return (h % 2) * 64

            pend = None
            pend_z = None
            pv_of_tile = {}
            for h in range(NH):
                hb = hb_of(h)
                fh = h // 2
                for st in range(4):
                    for qtl in range(4):
                        qt = st * 4 + qtl
                        S0 = 8 * (sb_ti % 2)
                        lhs = qbuf[hb:hb + 64,
                                   fh * SEQ + qt * 128: fh * SEQ + (qt + 1) * 128]
                        tscs = []
                        for n in range(4):
                            rhs = kbuf[hb:hb + 64, fh * SEQ + n * 512:
                                       fh * SEQ + (n + 1) * 512]
                            w0 = self.bw(n)
                            if n == 0 and proj_last is not None:
                                w0.append(proj_last)
                                proj_last = None
                            tscs.append(k.mms(
                                [lambda E, a=self.bank(n), l=lhs, r=rhs:
                                 E.matmul(a, l, r, start=True, stop=True)],
                                wait=w0))
                        if pend is not None:
                            pv_last, ttr = emit_tp(pend)
                            ph = pend[3]
                            pst = (pend[1]) // 4
                            p_free = ttr
                            if pend[1] % 4 == 3:
                                pend_z = (ph, pst, pv_last)
                            pend = None
                        if pend_z is not None:
                            last_zd = emit_zout(pend_z[0], pend_z[1], pend_z[2])
                            pend_z = None
                        for n in range(4):
                            k.op("vector", lambda E, n=n, S0=S0: E.reduce_max(
                                SM[:, S0 + n:S0 + n + 1], self.bank(n), axis=AX),
                                wait=[tscs[n]])
                        k.op("vector", lambda E, S0=S0: E.reduce_max(
                            SM[:, S0 + 4:S0 + 5], SM[:, S0:S0 + 4], axis=AX,
                            negate=True))
                        tneg = k.op("vector", lambda E, S0=S0:
                                    E.tensor_scalar_mul(
                            SM[:, S0 + 5:S0 + 6], SM[:, S0 + 4:S0 + 5], ISQ),
                            wait=[smrd[S0]])
                        texp = None
                        for n in range(4):
                            texp = k.op("scalar", lambda E, n=n, S0=S0:
                                        E.activation(
                                p_sb[:, n * 512:(n + 1) * 512], self.bank(n),
                                AF.Exp, bias=SM[:, S0 + 5:S0 + 6], scale=ISQ,
                                accum_out=SM[:, S0 + n:S0 + n + 1]),
                                wait=[tneg, p_free] if n == 0 else [tneg])
                            self.bset(n, texp)
                        smrd[S0] = texp
                        tsum4 = k.op("vector", lambda E, S0=S0: E.reduce_sum(
                            SM[:, S0 + 6:S0 + 7], SM[:, S0:S0 + 4], axis=AX),
                            wait=[texp])
                        trin = k.op("vector", lambda E, S0=S0: E.reciprocal(
                            SM[:, S0 + 7:S0 + 8], SM[:, S0 + 6:S0 + 7]),
                            wait=[tsum4])
                        nd = qt // 4
                        mvar = qt % 4
                        tm1 = k.op("vector", lambda E, nd=nd, mvar=mvar, S0=S0:
                                   E.scalar_tensor_tensor(
                            p_sb[:, nd * 512:(nd + 1) * 512],
                            p_sb[:, nd * 512:(nd + 1) * 512],
                            SM[:, S0 + 7:S0 + 8], self.l0diag(mvar),
                            ALU.mult, ALU.mult), wait=[trin])
                        tmasks = [tm1]
                        if nd < 3:
                            tm2 = k.op("vector", lambda E, nd=nd, S0=S0:
                                       E.tensor_scalar(
                                p_sb[:, (nd + 1) * 512:2048],
                                p_sb[:, (nd + 1) * 512:2048],
                                SM[:, S0 + 7:S0 + 8], None, ALU.mult),
                                wait=[trin])
                            tmasks.append(tm2)
                        pend = (tmasks, qt, hb, h, qtl == 0)
                        sb_ti += 1
            if pend is not None:
                pv_last, ttr = emit_tp(pend)
                p_free = ttr
                if pend[1] % 4 == 3:
                    pend_z = (pend[3], pend[1] // 4, pv_last)
                pend = None
            if pend_z is not None:
                last_zd = emit_zout(pend_z[0], pend_z[1], pend_z[2])
                pend_z = None

            prev_ta = None
            prev_fin = None
            for st in range(4):
                t0g = B + st * TG
                xgl = k.dma("sync", lambda E, t0g=t0g: E.dma_start(
                    self.XG[:, :].rearrange("p (c t) -> p c t", c=IC),
                    src[:, t0g:t0g + TG].rearrange("(c p) t -> p c t", p=128)),
                    wait=[self.xg_free, self.ln_prev.get("xg_read")])
                xnl = k.dma("sync", lambda E, t0g=t0g: E.dma_start(
                    self.F[:, 0:4096].rearrange("p (c t) -> p c t", c=IC),
                    self.xn_fm[:, t0g:t0g + TG].rearrange("(c p) t -> p c t", p=128)),
                    wait=[xn_wb, self.xn_free, prev_ta])
                zfl = k.dma("sync", lambda E, t0g=t0g, st=st: E.dma_start(
                    zf[:, :].rearrange("p (c t) -> p c t", c=IC),
                    self.z_scr[:, st * TG:(st + 1) * TG]
                    .rearrange("(c p) t -> p c t", p=128)),
                    wait=[last_zd, prev_fin])
                tadd = k.op("vector", lambda E: E.tensor_add(
                    self.F[:, 0:4096], self.F[:, 0:4096], zf),
                    wait=[xnl, zfl])
                ln2 = self.ln_group(None, 0, x_sbuf=self.F[:, 0:4096],
                                    x_sbuf_t=tadd, epilogue=False)
                r3 = lambda ap: ap.rearrange("p (c t) -> p c t", c=IC)
                ta = k.op("vector", lambda E: E.tensor_mul(
                    r3(zf), r3(self.F[:, 0:4096]), self.rep(ln2["bc_r_ps"], IC)),
                    wait=[ln2["tb1"], tadd])
                self.bset(5, ta)
                prev_ta = ta
                tb = k.op("vector", lambda E: E.tensor_sub(
                    r3(zf), r3(zf), self.rep(ln2["bc_m_ps"], IC)),
                    wait=[ln2["tb2"], ta])
                self.bset(6, tb)
                tg_last = tb
                for c in range(IC):
                    tg_last = k.op("vector", lambda E, c=c: E.tensor_scalar(
                        zf[:, c * 512:(c + 1) * 512], zf[:, c * 512:(c + 1) * 512],
                        self.col("gamma", c), self.col("beta", c),
                        ALU.mult, ALU.add), wait=[tg_last])
                tfin = k.op("vector", lambda E: E.tensor_add(
                    self.XG[:, :], self.XG[:, :], zf), wait=[tg_last, xgl])
                prev_fin = tfin
                self.xg_free = k.dma("gpsimd", lambda E, t0g=t0g: E.dma_start(
                    dst[:, t0g:t0g + TG].rearrange("(c p) t -> p c t", p=128),
                    self.XG[:, :].rearrange("p (c t) -> p c t", c=IC)),
                    wait=[tfin])
            wfree[0] = p_free
        k.end_phase()

    def phase_l1attn(self, src, dst):
        k = self.k
        k.begin_phase()
        self.ln_prev = {}
        self.xg_free = None
        self.xn_free = None
        self.bko = {b: None for b in range(8)}

        qbuf, kbuf, vbuf = self.BIG0, self.BIG1, self.BIG2
        wbuf = [self.W0, self.W1]
        p_sb = self.W0[:, 0:256]
        pT_sb = self.W0[:, 256:512]
        z_bf = self.LNS[:, 0:4096]
        SM = self.SMALL
        ISQ = 1.0 / 8.0

        prev_exp = None
        prev_rinv_read = None
        wfree = [None, None]
        for s in range(2):
            B = s * SEQ
            proj_last = None
            for g4 in range(4):
                t0 = B + g4 * TG
                toff = g4 * TG
                ln = self.ln_group(src, t0)
                xn_bf = ln["xn_bf"]
                xn_t = ln["xn_t"]
                for wname, dbuf, bcol in (("Wq1", qbuf, "bq1"), ("Wk1", kbuf, "bk1")):
                    for og in range(2):
                        bi = og % 2
                        dd = k.dma("sync", lambda E, b=wbuf[bi], og=og, wn=wname:
                                   E.dma_start(
                            b[:, :].rearrange("p (c o) -> p c o", c=IC),
                            self.wc[wn][:, og * 512:(og + 1) * 512]
                            .rearrange("(c p) o -> p c o", p=128)),
                            wait=[wfree[bi]])
                        for j in range(4):
                            oc = og * 4 + j
                            bk = oc % 4
                            fs = []
                            for c in range(IC):
                                lhs = wbuf[bi][:, c * 512 + j * 128:
                                               c * 512 + (j + 1) * 128]
                                rhs = xn_bf[:, c * TG:(c + 1) * TG]
                                fs.append(lambda E, a=self.bank(bk), l=lhs, r=rhs,
                                          st=(c == 0), sp=(c == IC - 1):
                                          E.matmul(a, l, r, start=st, stop=sp))
                            tm = k.mms(fs, wait=[dd, xn_t] + self.bw(bk))
                            te = k.op("vector", lambda E, oc=oc, bk=bk, dbuf=dbuf,
                                      bcol=bcol, toff=toff: E.tensor_scalar(
                                dbuf[:, oc * SEQ + toff: oc * SEQ + toff + TG],
                                self.bank(bk), self.col(bcol, oc), None, ALU.add),
                                wait=[tm])
                            self.bset(bk, te)
                            wfree[bi] = tm
                            proj_last = te
                for og in range(2):
                    bi = og % 2
                    dd = k.dma("sync", lambda E, b=wbuf[bi], og=og: E.dma_start(
                        b[:, :].rearrange("p (c o) -> p c o", c=IC),
                        self.wc["Wv1"][:, og * 512:(og + 1) * 512]
                        .rearrange("(c p) o -> p c o", p=128)), wait=[wfree[bi]])
                    for tt in range(4):
                        bk = tt % 4
                        fs = []
                        for c in range(IC):
                            lhs = xn_bf[:, c * TG + tt * 128: c * TG + (tt + 1) * 128]
                            rhs = wbuf[bi][:, c * TG:(c + 1) * TG]
                            fs.append(lambda E, a=self.bank(bk), l=lhs, r=rhs,
                                      st=(c == 0), sp=False:
                                      E.matmul(a, l, r, start=st, stop=sp))
                        fs.append(lambda E, a=self.bank(bk), og=og:
                                  E.matmul(a, self.ones_for("bv1"),
                                           self.row("bv1", og * 512, (og + 1) * 512),
                                           start=False, stop=True))
                        tm = k.mms(fs, wait=[dd, xn_t] + self.bw(bk))
                        tc128 = g4 * 4 + tt
                        te = k.op("vector", lambda E, tc=tc128, og=og, bk=bk:
                                  E.tensor_copy(
                            vbuf[:, tc * 1024 + og * 512: tc * 1024 + (og + 1) * 512],
                            self.bank(bk)), wait=[tm])
                        self.bset(bk, te)
                        wfree[bi] = tm
                        proj_last = te
                self.xn_free = wfree[1]

            p_free = wfree[0]
            zbf_frees = [wfree[0], wfree[1]]
            def emit_tp1(pend):
                tsc2, qt, kstart, kw, hb, h, qtl = pend
                nb = kw // 128
                ftr = []
                for i in range(nb):
                    ftr.append(lambda E, i=i: E.transpose(
                        self.PT[:, i * 128:(i + 1) * 128],
                        p_sb[:, i * 128:(i + 1) * 128], self.IDB[:, :]))
                ttr = k.mms(ftr, wait=[tsc2] + self.bw(7))
                pevac = k.op("vector", lambda E, nb=nb: E.tensor_copy(
                    pT_sb[:, 0:nb * 128], self.PT[:, 0:nb * 128]), wait=[ttr])
                self.bset(7, pevac)
                fpv = []
                kb0 = kstart // 128
                for i in range(nb):
                    fpv.append(lambda E, i=i, kb=kb0 + i, hb=hb, qtl=qtl,
                               h=h, sp=(i == nb - 1): E.matmul(
                        self.PS[hb:hb + 64,
                                4 * 512 + qtl * 128:4 * 512 + (qtl + 1) * 128],
                        vbuf[:, kb * 1024 + h * 64: kb * 1024 + h * 64 + 64],
                        pT_sb[:, i * 128:(i + 1) * 128],
                        start=(i == 0), stop=sp,
                        tile_position=(0, hb) if hb else None))
                wpv = [pevac]
                if qtl == 0:
                    wpv += self.bw(4)
                pv_last = k.mms(fpv, wait=wpv)
                return pv_last, ttr

            pend = None
            pend_z = None
            for st in range(4):
                z_done = []
                for h in range(NH):
                    hb = (h % 2) * 64
                    fh = h // 2
                    for qtl in range(4):
                        qt = st * 4 + qtl
                        first = (qt == 0)
                        kw = 128 if first else 256
                        kstart = 0 if first else qt * 128 - 128
                        lhs = qbuf[hb:hb + 64,
                                   fh * SEQ + qt * 128: fh * SEQ + (qt + 1) * 128]
                        rhs = kbuf[hb:hb + 64, fh * SEQ + kstart:
                                   fh * SEQ + kstart + kw]
                        bk = qtl % 4
                        w0 = self.bw(bk)
                        if proj_last is not None:
                            w0.append(proj_last)
                            proj_last = None
                        tsc = k.mms([lambda E, a=self.bank(bk, width=kw), l=lhs,
                                     r=rhs: E.matmul(a, l, r, start=True,
                                                     stop=True)], wait=w0)
                        if pend is not None:
                            pv_last, ttr = emit_tp1(pend)
                            p_free = ttr
                            if pend[6] == 3:
                                pend_z = (pend[5], pv_last)
                            pend = None
                        if pend_z is not None:
                            ph, ppv = pend_z
                            phb = (ph % 2) * 64
                            pfh = ph // 2
                            tz = k.op("vector", lambda E, hb=phb, fh=pfh:
                                      E.tensor_copy(
                                z_bf[hb:hb + 64, fh * 512:(fh + 1) * 512],
                                self.PS[hb:hb + 64, 4 * 512:4 * 512 + 512]),
                                wait=[ppv] + zbf_frees)
                            zbf_frees = []
                            self.bset(4, tz)
                            z_done.append(tz)
                            pend_z = None
                        tmax = k.op("vector", lambda E, bk=bk, kw=kw: E.reduce_max(
                            SM[:, 0:1], self.bank(bk, width=kw), axis=AX,
                            negate=True), wait=[tsc, prev_exp])
                        tneg = k.op("vector", lambda E: E.tensor_scalar_mul(
                            SM[:, 1:2], SM[:, 0:1], ISQ), wait=[tmax, prev_exp])
                        texp = k.op("scalar", lambda E, bk=bk, kw=kw: E.activation(
                            p_sb[:, 0:kw], self.bank(bk, width=kw), AF.Exp,
                            bias=SM[:, 1:2], scale=ISQ),
                            wait=[tneg, p_free, prev_rinv_read])
                        self.bset(bk, texp)
                        prev_exp = texp
                        mk = (self.mask("l1band", 256)[:, 128:256] if first
                              else self.mask("l1band", 256))
                        tmsk = k.op("vector", lambda E, kw=kw, mk=mk:
                                    E.scalar_tensor_tensor(
                            p_sb[:, 0:kw], p_sb[:, 0:kw], 1.0, mk,
                            ALU.mult, ALU.mult, accum_out=SM[:, 2:3]),
                            wait=[texp])
                        trin = k.op("vector", lambda E: E.reciprocal(
                            SM[:, 3:4], SM[:, 2:3]), wait=[tmsk])
                        tsc2 = k.op("vector", lambda E, kw=kw: E.tensor_scalar(
                            p_sb[:, 0:kw], p_sb[:, 0:kw], SM[:, 3:4], None,
                            ALU.mult), wait=[trin])
                        prev_rinv_read = tsc2
                        pend = (tsc2, qt, kstart, kw, hb, h, qtl)
                if pend is not None:
                    pv_last, ttr = emit_tp1(pend)
                    p_free = ttr
                    if pend[6] == 3:
                        pend_z = (pend[5], pv_last)
                    pend = None
                if pend_z is not None:
                    ph, ppv = pend_z
                    phb = (ph % 2) * 64
                    pfh = ph // 2
                    tz = k.op("vector", lambda E, hb=phb, fh=pfh: E.tensor_copy(
                        z_bf[hb:hb + 64, fh * 512:(fh + 1) * 512],
                        self.PS[hb:hb + 64, 4 * 512:4 * 512 + 512]),
                        wait=[ppv] + zbf_frees)
                    zbf_frees = []
                    self.bset(4, tz)
                    z_done.append(tz)
                    pend_z = None

                t0g = B + st * TG
                xgl = k.dma("sync", lambda E, t0g=t0g: E.dma_start(
                    self.XG[:, :].rearrange("p (c t) -> p c t", c=IC),
                    src[:, t0g:t0g + TG].rearrange("(c p) t -> p c t", p=128)),
                    wait=[self.xg_free])
                wo_last = None
                for og in range(2):
                    dd = k.dma("sync", lambda E, og=og: E.dma_start(
                        self.W1[:, :].rearrange("p (c o) -> p c o", c=IC),
                        self.wc["Wo1"][:, og * 512:(og + 1) * 512]
                        .rearrange("(c p) o -> p c o", p=128)),
                        wait=[wfree[1], wo_last])
                    for j in range(4):
                        oc = og * 4 + j
                        bk = oc % 4
                        fs = []
                        for c in range(IC):
                            lhs = self.W1[:, c * 512 + j * 128:
                                          c * 512 + (j + 1) * 128]
                            rhs = z_bf[:, c * 512:(c + 1) * 512]
                            fs.append(lambda E, a=self.bank(bk), l=lhs, r=rhs,
                                      st=(c == 0), sp=(c == IC - 1):
                                      E.matmul(a, l, r, start=st, stop=sp))
                        tm = k.mms(fs, wait=[dd, z_done[-1]] + self.bw(bk))
                        wo_last = tm
                        te = k.op("vector", lambda E, oc=oc, bk=bk:
                                  E.scalar_tensor_tensor(
                            self.XG[:, oc * 512:(oc + 1) * 512], self.bank(bk),
                            self.col("bo1", oc), self.XG[:, oc * 512:(oc + 1) * 512],
                            ALU.add, ALU.add), wait=[tm, xgl])
                        self.bset(bk, te)
                        xg_last = te
                wfree[1] = wo_last
                self.xg_free = k.dma("gpsimd", lambda E, t0g=t0g: E.dma_start(
                    dst[:, t0g:t0g + TG].rearrange("(c p) t -> p c t", p=128),
                    self.XG[:, :].rearrange("p (c t) -> p c t", c=IC)),
                    wait=[xg_last])
                zbf_frees = [wo_last]
            wfree[0] = p_free
        k.end_phase()

    def phase_l1ffn(self, src, dst, app, debug=None, no_scatter=False):
        k = self.k
        k.begin_phase()
        self.ln_prev = {}
        self.xg_free = None
        self.xn_free = None
        self.bko = {b: None for b in range(8)}
        corr = self.corr0_d if app == 0 else self.corr1_d
        SM = self.SMALL

        DFC = 22
        last_scol_read = None
        wbuf = [self.W0, self.W1]
        wfree = [None, None]
        prev_scat = None
        prev_gather = None
        for b in range(2):
            B = b * SEQ
            xn_wb_last = None
            scol_evacs = []
            srow_evacs = []
            for g in range(4):
                t0 = B + g * TG
                ln = self.ln_group(src, t0, exact=True, want_fp32=True,
                                   xe_buf=self.BIG2[:, 0:4096],
                                   sqe_buf=self.BIG2[:, 4096:8192])
                self.ln_prev["txe_free"] = ln["tsum"]
                self.ln_prev["sqe_free"] = ln["tssq"]
                xnf = ln["TMP"]
                xnf_t = ln["xn_f32_t"]
                xg = ln["xg"]

                frow = []
                for c in range(IC):
                    frow.append(lambda E, c=c: E.matmul(
                        self.PS[0:1, 4 * 512:4 * 512 + 512],
                        self.col("wr", c), xnf[:, c * TG:(c + 1) * TG],
                        start=(c == 0), stop=(c == IC - 1)))
                trow = k.mms(frow, wait=[xnf_t] + self.bw(4))
                tse = k.op("scalar", lambda E, g=g: E.activation(
                    self.SROW[0:1, g * 512:(g + 1) * 512],
                    self.PS[0:1, 4 * 512:4 * 512 + 512], AF.Copy), wait=[trow])
                self.bset(4, tse)
                srow_evacs.append(tse)
                fcol = []
                for tt in range(4):
                    fcol.append(lambda E, tt=tt, g=g: E.transpose(
                        self.PS[0:128, 5 * 512 + tt:5 * 512 + tt + 1],
                        self.SROW[0:1, (g * 4 + tt) * 128:(g * 4 + tt + 1) * 128],
                        self.IDF[0:1, 0:1]))
                tcol = k.mms(fcol, wait=[tse] + self.bw(5))
                tce = k.op("vector", lambda E, g=g: E.tensor_copy(
                    self.SCOL[:, g * 4:(g + 1) * 4],
                    self.PS[0:128, 5 * 512:5 * 512 + 4]),
                    wait=[tcol, last_scol_read])
                self.bset(5, tce)
                scol_evacs.append(tce)

                tup = k.op("vector", lambda E: E.scalar_tensor_tensor(
                    xg[:, :], xnf[:, :], 0.5, xg[:, :], ALU.mult, ALU.add),
                    wait=[xnf_t, ln["xg_t"]])
                self.xg_free = k.dma("gpsimd", lambda E, t0=t0: E.dma_start(
                    dst[:, t0:t0 + TG].rearrange("(c p) t -> p c t", p=128),
                    xg[:, :].rearrange("p (c t) -> p c t", c=IC)), wait=[tup])

                tmstage = self.F[:, 4096:5120]
                prev_ev = self.ln_prev.get("sqf_free")
                for tt in range(4):
                    evs = []
                    for half in range(2):
                        ftr = []
                        for i in range(4):
                            fc = half * 4 + i
                            ftr.append(lambda E, i=i, fc=fc, tt=tt: E.transpose(
                                self.PS[:, 6 * 512 + i * 128:6 * 512 + (i + 1) * 128],
                                xnf[:, fc * 512 + tt * 128: fc * 512 + (tt + 1) * 128],
                                self.IDF[:, :]))
                        ttr = k.mms(ftr, wait=[xnf_t] + self.bw(6))
                        tev = k.op("vector", lambda E, half=half: E.tensor_copy(
                            tmstage[:, half * 512:(half + 1) * 512],
                            self.PS[:, 6 * 512:6 * 512 + 512]),
                            wait=[ttr, prev_ev])
                        self.bset(6, tev)
                        evs.append(tev)
                    prev_ev = k.dma("gpsimd", lambda E, t0=t0, tt=tt: E.dma_start(
                        self.xn_tm[t0 + tt * 128: t0 + (tt + 1) * 128, :],
                        tmstage[:, :]), wait=evs)
                    xn_wb_last = prev_ev
                self.ln_prev["sqf_free"] = prev_ev
                self.ln_prev["tmp_free"] = ttr
                self.ln_prev["tmp_free"] = ttr

            s_bc = self.F[:, 0:2048]
            iota_bc = self.F[:, 2048:4096]
            tio = k.op("gpsimd", lambda E: E.iota(
                iota_bc, [[1, 2048]], channel_multiplier=0,
                allow_small_or_imprecise_dtypes=True),
                wait=[self.ln_prev["tmp_free"], xn_wb_last])
            fbc = []
            bce = []
            for n in range(4):
                tbc = k.mms([lambda E, n=n: E.matmul(
                    self.bank(n), self.row("ones", 0, 128),
                    self.SROW[0:1, n * 512:(n + 1) * 512], start=True, stop=True)],
                    wait=[srow_evacs[-1]] + self.bw(n))
                te = k.op("vector", lambda E, n=n: E.tensor_copy(
                    s_bc[:, n * 512:(n + 1) * 512], self.bank(n)),
                    wait=[tbc, self.ln_prev["tmp_free"]])
                self.bset(n, te)
                bce.append(te)
            gt_dump = self.BIG0[:, 0:2048]
            eq_dump = self.BIG0[:, 2048:4096]
            rank_done = []
            prev_tie = None
            for ic in range(16):
                tgt = k.op("vector", lambda E, ic=ic: E.tensor_scalar(
                    gt_dump, s_bc, self.SCOL[:, ic:ic + 1], 0.0,
                    ALU.is_gt, ALU.add, accum_out=self.RNK[:, ic:ic + 1]),
                    wait=[bce[-1], scol_evacs[-1]])
                teq = k.op("vector", lambda E, ic=ic: E.tensor_scalar(
                    eq_dump, s_bc, self.SCOL[:, ic:ic + 1], None, ALU.is_equal),
                    wait=[tgt])
                ttie = k.op("vector", lambda E, ic=ic: E.scalar_tensor_tensor(
                    eq_dump, iota_bc, self.col("iota", ic), eq_dump,
                    ALU.is_lt, ALU.logical_and, accum_out=SM[:, 4:5]),
                    wait=[teq, tio, prev_tie])
                tfin = k.op("vector", lambda E, ic=ic: E.tensor_add(
                    self.RNK[:, ic:ic + 1], self.RNK[:, ic:ic + 1], SM[:, 4:5]),
                    wait=[ttie])
                prev_tie = tfin
                rank_done.append(tfin)

            ot = [self.F[:, 4096:4224], self.F[:, 4224:4352]]
            idx_done = []
            for pc in range(4):
                fmm_t = None
                for ic in range(16):
                    o = ot[ic % 2]
                    tcmp = k.op("vector", lambda E, pc=pc, ic=ic, o=o:
                                E.tensor_scalar(
                        o, iota_bc[:, pc * 128:(pc + 1) * 128],
                        self.RNK[:, ic:ic + 1], None, ALU.is_equal),
                        wait=[rank_done[ic], fmm_t, self.ln_prev.get("sqf_free")
                              if pc == 0 and ic < 2 else None])
                    fmm_t = k.mms([lambda E, pc=pc, ic=ic, o=o: E.matmul(
                        self.PS[0:128, 5 * 512 + 8 + pc:5 * 512 + 8 + pc + 1],
                        o, self.col("iota", ic),
                        start=(ic == 0), stop=(ic == 15))],
                        wait=[tcmp] + (self.bw(5) if ic == 0 else []))
                tixa = k.op("vector", lambda E, pc=pc, B=B: E.tensor_scalar(
                    SM[:, 12:13],
                    self.PS[0:128, 5 * 512 + 8 + pc:5 * 512 + 8 + pc + 1],
                    float(B), float(B + SEQ - 1), ALU.add, ALU.min),
                    wait=[fmm_t, prev_scat, prev_gather, self.ln_prev.get("tix_sm")])
                tix = k.op("vector", lambda E, pc=pc, B=B: E.tensor_scalar(
                    self.IDXC[:, pc:pc + 1], SM[:, 12:13],
                    float(B), None, ALU.max), wait=[tixa])
                self.ln_prev["tix_sm"] = tix
                self.bset(5, tix)
                idx_done.append(tix)
            last_scol_read = rank_done[-1]

            if debug is not None:
                dsm = debug["small"]
                k.dma("sync", lambda E, b=b, dsm=dsm: E.dma_start(
                    dsm[0:128, b * 40:b * 40 + 16], self.RNK[:, :]),
                    wait=[rank_done[-1]])
                k.dma("sync", lambda E, b=b, dsm=dsm: E.dma_start(
                    dsm[0:128, b * 40 + 16: b * 40 + 32], self.SCOL[:, :]),
                    wait=[rank_done[-1]])
                k.dma("gpsimd", lambda E, b=b, dsm=dsm: E.dma_start(
                    dsm[0:128, b * 40 + 32:b * 40 + 36], self.IDXC[:, :]),
                    wait=[idx_done[-1]])
            x_sel = self.F[:, 0:4096]
            gather_t = []
            for pc in range(4):
                gt = k.dma("gpsimd", lambda E, pc=pc: E.indirect_dma_start(
                    x_sel[:, pc * 1024:(pc + 1) * 1024], None,
                    self.xn_tm[0:NT, :],
                    bass.IndirectOffsetOnAxis(self.IDXC[:, pc:pc + 1], 0)),
                    wait=[idx_done[-1], bce[-1], rank_done[-1]])
                gather_t.append(gt)

            if debug is not None and b == 0:
                k.dma("sync", lambda E: E.dma_start(
                    debug["big"][0:128, 0:1024], x_sel[:, 0:1024]),
                    wait=[gather_t[0]])
            sq_dump = self.BIG0[:, 0:1024]
            rms_sqs = []
            for pc in range(4):
                tsq = k.op("scalar", lambda E, pc=pc: E.activation(
                    sq_dump, x_sel[:, pc * 1024:(pc + 1) * 1024], AF.Square,
                    accum_out=SM[:, pc:pc + 1]), wait=[gather_t[pc], prev_tie])
                rms_sqs.append(tsq)

            rstds = []
            for pc in range(4):
                tv = k.op("vector", lambda E, pc=pc: E.tensor_scalar(
                    SM[:, 8 + pc:9 + pc], SM[:, pc:pc + 1], 1.0 / D, RMS_EPS,
                    ALU.mult, ALU.add), wait=[rms_sqs[pc]])
                ts = k.op("scalar", lambda E, pc=pc: E.activation(
                    SM[:, 8 + pc:9 + pc], SM[:, 8 + pc:9 + pc], AF.Sqrt),
                    wait=[tv])
                tr = k.op("vector", lambda E, pc=pc: E.reciprocal(
                    SM[:, 8 + pc:9 + pc], SM[:, 8 + pc:9 + pc]), wait=[ts])
                rstds.append(tr)

            def gbrd(row_name):
                f1 = k.mms([lambda E: E.matmul(
                    self.PS[:, 5 * 512:6 * 512], self.ones_for(row_name),
                    self.row(row_name, 0, 512), start=True, stop=True)],
                    wait=self.bw(5))
                f2 = k.mms([lambda E: E.matmul(
                    self.PS[:, 6 * 512:7 * 512], self.ones_for(row_name),
                    self.row(row_name, 512, 1024), start=True, stop=True)],
                    wait=self.bw(6))
                return f1, f2

            tg1, tg2 = gbrd("g1")
            h_tm = self.BIG0[:, 0:4096]
            h_tm_t = []
            for pc in range(4):
                t = k.op("vector", lambda E, pc=pc: E.scalar_tensor_tensor(
                    h_tm[:, pc * 1024:(pc + 1) * 1024],
                    x_sel[:, pc * 1024:(pc + 1) * 1024],
                    SM[:, 8 + pc:9 + pc], self.PS[:, 5 * 512:5 * 512 + 1024],
                    ALU.mult, ALU.mult),
                    wait=[rstds[pc], tg1, tg2, rms_sqs[-1]])
                h_tm_t.append(t)
            self.bset(5, h_tm_t[-1])
            self.bset(6, h_tm_t[-1])

            def tm_to_fm2(src_tm, dst_fm, src_ts, dtw=512):
                done = []
                for pc in range(4):
                    for half in range(2):
                        ftr = []
                        for i in range(4):
                            fc = half * 4 + i
                            ftr.append(lambda E, i=i, fc=fc, pc=pc: E.transpose(
                                self.PT[:, i * 128:(i + 1) * 128],
                                src_tm[:, pc * 1024 + fc * 128:
                                       pc * 1024 + (fc + 1) * 128],
                                self.IDB[:, :]))
                        ttr = k.mms(ftr, wait=[src_ts[pc]] + self.bw(7))
                        tevs = None
                        for i in range(4):
                            fc = half * 4 + i
                            tevs = k.op("vector", lambda E, i=i, fc=fc, pc=pc:
                                        E.tensor_copy(
                                dst_fm[:, fc * dtw + pc * 128:
                                       fc * dtw + (pc + 1) * 128],
                                self.PT[:, i * 128:(i + 1) * 128]), wait=[ttr])
                        self.bset(7, tevs)
                        done.append(tevs)
                return done

            h_fm = self.BIG0[:, 4096:8192]
            hfm_done = tm_to_fm2(h_tm, h_fm, h_tm_t)

            qf = self.BIG1[:, 0:4096]
            kf = self.BIG1[:, 4096:8192]
            vt = self.BIG1[:, 8192:12288]
            proj_last = None
            for part, dbuf in (("q", qf), ("k", kf)):
                obase = 0 if part == "q" else 1024
                for og in range(2):
                    bi = og % 2
                    dd = k.dma("sync", lambda E, b=wbuf[bi], og=og, ob=obase:
                               E.dma_start(
                        b[:, :].rearrange("p (c o) -> p c o", c=IC),
                        self.wc["WqkvT"][:, ob + og * 512: ob + (og + 1) * 512]
                        .rearrange("(c p) o -> p c o", p=128)),
                        wait=[wfree[bi]])
                    for j in range(4):
                        oc = og * 4 + j
                        bk = oc % 4
                        fs = []
                        for c in range(IC):
                            lhs = wbuf[bi][:, c * 512 + j * 128:
                                           c * 512 + (j + 1) * 128]
                            rhs = h_fm[:, c * 512:(c + 1) * 512]
                            fs.append(lambda E, a=self.bank(bk), l=lhs, r=rhs,
                                      st=(c == 0), sp=(c == IC - 1):
                                      E.matmul(a, l, r, start=st, stop=sp))
                        tm = k.mms(fs, wait=[dd, hfm_done[-1]] + self.bw(bk))
                        te = k.op("vector", lambda E, oc=oc, bk=bk, dbuf=dbuf:
                                  E.tensor_copy(
                            dbuf[:, oc * 512:(oc + 1) * 512], self.bank(bk)),
                            wait=[tm])
                        self.bset(bk, te)
                        wfree[bi] = tm
                        proj_last = te
            for og in range(2):
                bi = og % 2
                dd = k.dma("sync", lambda E, b=wbuf[bi], og=og: E.dma_start(
                    b[:, :].rearrange("p (c o) -> p c o", c=IC),
                    self.wc["WqkvT"][:, 2048 + og * 512: 2048 + (og + 1) * 512]
                    .rearrange("(c p) o -> p c o", p=128)), wait=[wfree[bi]])
                for tt in range(4):
                    bk = tt % 4
                    fs = []
                    for c in range(IC):
                        lhs = h_fm[:, c * 512 + tt * 128: c * 512 + (tt + 1) * 128]
                        rhs = wbuf[bi][:, c * 512:(c + 1) * 512]
                        fs.append(lambda E, a=self.bank(bk), l=lhs, r=rhs,
                                  st=(c == 0), sp=(c == IC - 1):
                                  E.matmul(a, l, r, start=st, stop=sp))
                    tm = k.mms(fs, wait=[dd, hfm_done[-1]] + self.bw(bk))
                    te = k.op("vector", lambda E, tt=tt, og=og, bk=bk:
                              E.tensor_copy(
                        vt[:, tt * 1024 + og * 512: tt * 1024 + (og + 1) * 512],
                        self.bank(bk)), wait=[tm])
                    self.bset(bk, te)
                    wfree[bi] = tm
                    proj_last = te

            p_sb = self.ASL
            pT_sb = self.W0[:, 0:512]
            z_bf = self.LNS[:, 0:4096]
            pz_free = [wfree[0], proj_last]
            prev_exp2 = None
            prev_rinv2 = None
            z_done = []
            for h in range(NH):
                hb = (h % 2) * 64
                fh = h // 2
                pv_last = None
                for qt in range(4):
                    kw = (qt + 1) * 128
                    lhs = qf[hb:hb + 64, fh * 512 + qt * 128:
                             fh * 512 + (qt + 1) * 128]
                    rhs = kf[hb:hb + 64, fh * 512: fh * 512 + kw]
                    bk = qt % 4
                    tsc = k.mms([lambda E, a=self.bank(bk, width=kw), l=lhs,
                                 r=rhs: E.matmul(a, l, r, start=True, stop=True)],
                                wait=self.bw(bk) + ([proj_last] if h == 0 and
                                                    qt == 0 else []))
                    tmax = k.op("vector", lambda E, bk=bk, kw=kw: E.reduce_max(
                        SM[:, 0:1], self.bank(bk, width=kw), axis=AX, negate=True),
                        wait=[tsc, prev_exp2])
                    tneg = k.op("vector", lambda E: E.tensor_scalar_mul(
                        SM[:, 1:2], SM[:, 0:1], 1.0 / 8.0), wait=[tmax, prev_exp2])
                    texp = k.op("scalar", lambda E, bk=bk, kw=kw: E.activation(
                        p_sb[:, 0:kw], self.bank(bk, width=kw), AF.Exp,
                        bias=SM[:, 1:2], scale=1.0 / 8.0),
                        wait=[tneg, pz_free[0], pz_free[1] if len(pz_free) > 1
                              else None, prev_rinv2])
                    pz_free = [None]
                    self.bset(bk, texp)
                    prev_exp2 = texp
                    tmsk = k.op("vector", lambda E, qt=qt: E.scalar_tensor_tensor(
                        p_sb[:, qt * 128:(qt + 1) * 128],
                        p_sb[:, qt * 128:(qt + 1) * 128], 1.0,
                        self.mask("l1band", 256)[:, 128:256],
                        ALU.mult, ALU.mult), wait=[texp])
                    tsum = k.op("vector", lambda E, kw=kw: E.reduce_sum(
                        SM[:, 2:3], p_sb[:, 0:kw], axis=AX), wait=[tmsk])
                    trin = k.op("vector", lambda E: E.reciprocal(
                        SM[:, 3:4], SM[:, 2:3]), wait=[tsum])
                    tsc2 = k.op("vector", lambda E, kw=kw: E.tensor_scalar(
                        p_sb[:, 0:kw], p_sb[:, 0:kw], SM[:, 3:4], None, ALU.mult),
                        wait=[trin])
                    prev_rinv2 = tsc2
                    nb = qt + 1
                    ftr = []
                    for i in range(nb):
                        ftr.append(lambda E, i=i: E.transpose(
                            self.PT[:, i * 128:(i + 1) * 128],
                            p_sb[:, i * 128:(i + 1) * 128], self.IDB[:, :]))
                    ttr = k.mms(ftr, wait=[tsc2] + self.bw(7))
                    pevac = k.op("vector", lambda E, nb=nb: E.tensor_copy(
                        pT_sb[:, 0:nb * 128], self.PT[:, 0:nb * 128]), wait=[ttr])
                    self.bset(7, pevac)
                    fpv = []
                    for i in range(nb):
                        fpv.append(lambda E, i=i, hb=hb, qt=qt, h=h,
                                   sp=(i == nb - 1): E.matmul(
                            self.PS[hb:hb + 64,
                                    4 * 512 + qt * 128:4 * 512 + (qt + 1) * 128],
                            vt[:, i * 1024 + h * 64: i * 1024 + h * 64 + 64],
                            pT_sb[:, i * 128:(i + 1) * 128],
                            start=(i == 0), stop=sp,
                            tile_position=(0, hb) if hb else None))
                    wpv = [pevac]
                    if qt == 0:
                        wpv += self.bw(4)
                    pv_last = k.mms(fpv, wait=wpv)
                    pz_free = [ttr]
                tz = k.op("vector", lambda E, hb=hb, fh=fh: E.tensor_copy(
                    z_bf[hb:hb + 64, fh * 512:(fh + 1) * 512],
                    self.PS[hb:hb + 64, 4 * 512:4 * 512 + 512]),
                    wait=[pv_last])
                self.bset(4, tz)
                z_done.append(tz)

            if debug is not None and b == 0:
                k.dma("gpsimd", lambda E: E.dma_start(
                    debug["big"][0:128, 1024:5120], z_bf[:, :]),
                    wait=[z_done[-1]])
            xat_fm = self.F[:, 4096:8192]
            wo_last = None
            xat_t = None
            for og in range(2):
                bi = og % 2
                dd = k.dma("sync", lambda E, b=wbuf[bi], og=og: E.dma_start(
                    b[:, :].rearrange("p (c o) -> p c o", c=IC),
                    self.wc["WoutT"][:, og * 512:(og + 1) * 512]
                    .rearrange("(c p) o -> p c o", p=128)),
                    wait=[wfree[bi], z_done[-1]])
                for j in range(4):
                    oc = og * 4 + j
                    bk = oc % 4
                    fs = []
                    for c in range(IC):
                        lhs = wbuf[bi][:, c * 512 + j * 128: c * 512 + (j + 1) * 128]
                        rhs = z_bf[:, c * 512:(c + 1) * 512]
                        fs.append(lambda E, a=self.bank(bk), l=lhs, r=rhs,
                                  st=(c == 0), sp=(c == IC - 1):
                                  E.matmul(a, l, r, start=st, stop=sp))
                    tm = k.mms(fs, wait=[dd, z_done[-1]] + self.bw(bk))
                    wo_last = tm
                    xat_t = k.op("scalar", lambda E, oc=oc, bk=bk: E.activation(
                        xat_fm[:, oc * 512:(oc + 1) * 512], self.bank(bk),
                        AF.Copy, scale=0.5), wait=[tm])
                    self.bset(bk, xat_t)
                wfree[bi] = wo_last

            xat_tm = self.XG
            xat_tm_t = []
            for pc in range(4):
                for half in range(2):
                    ftr = []
                    for i in range(4):
                        fc = half * 4 + i
                        ftr.append(lambda E, i=i, fc=fc, pc=pc: E.transpose(
                            self.PS[:, 6 * 512 + i * 128:6 * 512 + (i + 1) * 128],
                            xat_fm[:, fc * 512 + pc * 128: fc * 512 + (pc + 1) * 128],
                            self.IDF[:, :]))
                    ttr2 = k.mms(ftr, wait=[xat_t] + self.bw(6))
                    tev = None
                    for i in range(4):
                        fc = half * 4 + i
                        tev = k.op("vector", lambda E, i=i, fc=fc, pc=pc:
                                   E.tensor_copy(
                            xat_tm[:, pc * 1024 + fc * 128:
                                   pc * 1024 + (fc + 1) * 128],
                            self.PS[:, 6 * 512 + i * 128:6 * 512 + (i + 1) * 128]),
                            wait=[ttr2, self.xg_free if pc == 0 and half == 0
                                  else None])
                    self.bset(6, tev)
                    xat_tm_t.append(tev)

            ts2 = []
            for pc in range(4):
                t = k.op("vector", lambda E, pc=pc: E.scalar_tensor_tensor(
                    x_sel[:, pc * 1024:(pc + 1) * 1024],
                    xat_tm[:, pc * 1024:(pc + 1) * 1024], 2.0,
                    x_sel[:, pc * 1024:(pc + 1) * 1024], ALU.mult, ALU.add),
                    wait=[xat_tm_t[2 * pc + 1], h_tm_t[-1]])
                ts2.append(t)
            rms2_sqs = []
            for pc in range(4):
                tsq = k.op("scalar", lambda E, pc=pc: E.activation(
                    sq_dump, x_sel[:, pc * 1024:(pc + 1) * 1024], AF.Square,
                    accum_out=SM[:, pc:pc + 1]), wait=[ts2[pc], rstds[-1]])
                rms2_sqs.append(tsq)
            rstd2 = []
            for pc in range(4):
                tv = k.op("vector", lambda E, pc=pc: E.tensor_scalar(
                    SM[:, 8 + pc:9 + pc], SM[:, pc:pc + 1], 1.0 / D, RMS_EPS,
                    ALU.mult, ALU.add), wait=[rms2_sqs[pc], h_tm_t[-1]])
                ts = k.op("scalar", lambda E, pc=pc: E.activation(
                    SM[:, 8 + pc:9 + pc], SM[:, 8 + pc:9 + pc], AF.Sqrt),
                    wait=[tv])
                tr = k.op("vector", lambda E, pc=pc: E.reciprocal(
                    SM[:, 8 + pc:9 + pc], SM[:, 8 + pc:9 + pc]), wait=[ts])
                rstd2.append(tr)
            tg1b, tg2b = gbrd("g2")
            h2_tm = self.BIG0[:, 0:4096]
            h2_tm_t = []
            for pc in range(4):
                t = k.op("vector", lambda E, pc=pc: E.scalar_tensor_tensor(
                    h2_tm[:, pc * 1024:(pc + 1) * 1024],
                    x_sel[:, pc * 1024:(pc + 1) * 1024],
                    SM[:, 8 + pc:9 + pc], self.PS[:, 5 * 512:5 * 512 + 1024],
                    ALU.mult, ALU.mult),
                    wait=[rstd2[pc], tg1b, tg2b, rms2_sqs[-1]])
                h2_tm_t.append(t)
            self.bset(5, h2_tm_t[-1])
            self.bset(6, h2_tm_t[-1])
            h2_fm = self.BIG0[:, 4096:8192]
            h2fm_done = tm_to_fm2(h2_tm, h2_fm, h2_tm_t)

            gg = self.BIG1[:, 0:DFC * 512]
            last_te2 = None
            gg_done = None
            for hc2 in range(DFC):
                ow = 128 if hc2 < 21 else DFF - 21 * 128
                bi = hc2 % 2
                base = 0 if bi == 0 else 0
                d1 = k.dma("sync", lambda E, b=wbuf[bi], hc2=hc2, ow=ow:
                           E.dma_start(
                    b[0:128, 0:IC * ow].rearrange("p (c o) -> p c o", c=IC),
                    self.wc["Wf1T"][:, hc2 * 128: hc2 * 128 + ow]
                    .rearrange("(c p) o -> p c o", p=128)), wait=[wfree[bi]])
                d2 = k.dma("sync", lambda E, b=wbuf[bi], hc2=hc2, ow=ow:
                           E.dma_start(
                    b[0:128, 2048:2048 + IC * ow].rearrange(
                        "p (c o) -> p c o", c=IC),
                    self.wc["Wf2T"][:, hc2 * 128: hc2 * 128 + ow]
                    .rearrange("(c p) o -> p c o", p=128)))
                ba, bb = (0, 1) if hc2 % 2 == 0 else (2, 3)
                fa, fb = [], []
                for c in range(IC):
                    lhs1 = wbuf[bi][:, c * ow: (c + 1) * ow]
                    lhs2 = wbuf[bi][:, 2048 + c * ow: 2048 + (c + 1) * ow]
                    rhs = h2_fm[:, c * 512:(c + 1) * 512]
                    fa.append(lambda E, a=self.bank(ba, parts=ow), l=lhs1, r=rhs,
                              st=(c == 0), sp=(c == IC - 1):
                              E.matmul(a, l, r, start=st, stop=sp))
                    fb.append(lambda E, a=self.bank(bb, parts=ow), l=lhs2, r=rhs,
                              st=(c == 0), sp=(c == IC - 1):
                              E.matmul(a, l, r, start=st, stop=sp))
                ta = k.mms(fa, wait=[d2, h2fm_done[-1]] + self.bw(ba, bb))
                tb = k.mms(fb)
                te1 = k.op("scalar", lambda E, ba=ba, ow=ow: E.activation(
                    self.ASL[0:ow, :], self.bank(ba, parts=ow), AF.Silu),
                    wait=[ta, last_te2])
                te2 = k.op("vector", lambda E, hc2=hc2, bb=bb, ow=ow:
                           E.scalar_tensor_tensor(
                    gg[0:ow, hc2 * 512:(hc2 + 1) * 512],
                    self.bank(bb, parts=ow), 1.0, self.ASL[0:ow, :],
                    ALU.mult, ALU.mult), wait=[tb, te1])
                self.bset(ba, te1)
                self.bset(bb, te2)
                last_te2 = te2
                wfree[bi] = tb
                gg_done = te2

            c_fm_t = None
            for oc in range(IC):
                bi = oc % 2
                d1 = k.dma("sync", lambda E, b=wbuf[bi], oc=oc: E.dma_start(
                    b[:, 0:21 * 128].rearrange("p (c o) -> p c o", c=21),
                    self.wc["Wf3T"][0:21 * 128, oc * 128:(oc + 1) * 128]
                    .rearrange("(c p) o -> p c o", p=128)), wait=[wfree[bi]])
                d2 = k.dma("sync", lambda E, b=wbuf[bi], oc=oc: E.dma_start(
                    b[0:42, 21 * 128:22 * 128],
                    self.wc["Wf3T"][21 * 128:DFF, oc * 128:(oc + 1) * 128]))
                bk = oc % 4
                fs = []
                for c in range(DFC):
                    cw = 128 if c < 21 else DFF - 21 * 128
                    lhs = wbuf[bi][0:cw, c * 128:c * 128 + 128]
                    rhs = gg[0:cw, c * 512:(c + 1) * 512]
                    fs.append(lambda E, a=self.bank(bk), l=lhs, r=rhs,
                              st=(c == 0), sp=(c == DFC - 1):
                              E.matmul(a, l, r, start=st, stop=sp))
                tm = k.mms(fs, wait=[d2, gg_done] + self.bw(bk))
                wfree[bi] = tm
                c_fm_t = k.op("vector", lambda E, oc=oc, bk=bk:
                              E.scalar_tensor_tensor(
                    xat_fm[:, oc * 512:(oc + 1) * 512], self.bank(bk), 0.5,
                    xat_fm[:, oc * 512:(oc + 1) * 512], ALU.mult, ALU.add),
                    wait=[tm, ts2[-1]])
                self.bset(bk, c_fm_t)

            if debug is not None and b == 0:
                k.dma("sync", lambda E: E.dma_start(
                    debug["big"][0:128, 5120:9216], xat_fm[:, :]),
                    wait=[c_fm_t])
                k.dma("sync", lambda E: E.dma_start(
                    debug["big"][0:128, 9216:13312], self.F[:, 0:4096]),
                    wait=[ts2[-1]])
            c_tm = self.XG
            ctm_t = []
            for pc in range(4):
                for half in range(2):
                    ftr = []
                    for i in range(4):
                        fc = half * 4 + i
                        ftr.append(lambda E, i=i, fc=fc, pc=pc: E.transpose(
                            self.PS[:, 6 * 512 + i * 128:6 * 512 + (i + 1) * 128],
                            xat_fm[:, fc * 512 + pc * 128: fc * 512 + (pc + 1) * 128],
                            self.IDF[:, :]))
                    ttr3 = k.mms(ftr, wait=[c_fm_t] + self.bw(6))
                    tev = None
                    for i in range(4):
                        fc = half * 4 + i
                        tev = k.op("vector", lambda E, i=i, fc=fc, pc=pc:
                                   E.tensor_copy(
                            c_tm[:, pc * 1024 + fc * 128:
                                 pc * 1024 + (fc + 1) * 128],
                            self.PS[:, 6 * 512 + i * 128:6 * 512 + (i + 1) * 128]),
                            wait=[ttr3, ts2[-1] if pc == 0 and half == 0 else None])
                    self.bset(6, tev)
                    ctm_t.append(tev)
            if debug is not None:
                dsm = debug["small"]
                k.dma("gpsimd", lambda E, b=b, dsm=dsm: E.dma_start(
                    dsm[0:128, b * 40 + 36:b * 40 + 40], self.IDXC[:, :]),
                    wait=[ctm_t[-1]])
            scat_t = []
            for pc in range(4):
                if no_scatter:
                    scat_t.append(ctm_t[2 * pc + 1])
                    continue
                stx = k.dma("gpsimd", lambda E, pc=pc: E.indirect_dma_start(
                    corr[0:NT, :],
                    bass.IndirectOffsetOnAxis(self.IDXC[:, pc:pc + 1], 0),
                    c_tm[:, pc * 1024:(pc + 1) * 1024], None),
                    wait=[ctm_t[2 * pc + 1], idx_done[pc]])
                scat_t.append(stx)
            prev_scat = scat_t[-1]
            prev_gather = gather_t[-1]

            prev_tr4 = None
            for g in range(4):
                t0 = B + g * TG
                ctiles = self.F[:, 0:4096]
                tld = []
                for tt in range(4):
                    tl = k.dma("sync", lambda E, t0=t0, tt=tt: E.dma_start(
                        ctiles[:, tt * 1024:(tt + 1) * 1024],
                        corr[t0 + tt * 128: t0 + (tt + 1) * 128, :]),
                        wait=[scat_t[-1], prev_tr4] +
                        ([gather_t[-1]] if g == 0 else []))
                    tld.append(tl)
                xgl = k.dma("gpsimd", lambda E, t0=t0: E.dma_start(
                    self.XG[:, :].rearrange("p (c t) -> p c t", c=IC),
                    dst[:, t0:t0 + TG].rearrange("(c p) t -> p c t", p=128)),
                    wait=[self.xg_free, ctm_t[-1], scat_t[-1]])
                cfm = self.F[:, 4096:8192]
                cfm_t = None
                for pc in range(4):
                    for half in range(2):
                        ftr = []
                        for i in range(4):
                            fc = half * 4 + i
                            ftr.append(lambda E, i=i, fc=fc, pc=pc: E.transpose(
                                self.PS[:, 6 * 512 + i * 128:
                                        6 * 512 + (i + 1) * 128],
                                ctiles[:, pc * 1024 + fc * 128:
                                       pc * 1024 + (fc + 1) * 128],
                                self.IDF[:, :]))
                        ttr4 = k.mms(ftr, wait=[tld[pc], c_fm_t if g == 0 and
                                                pc == 0 and half == 0 else None]
                                     + self.bw(6))
                        prev_tr4 = ttr4
                        tev = None
                        for i in range(4):
                            fc = half * 4 + i
                            tev = k.op("vector", lambda E, i=i, fc=fc, pc=pc:
                                       E.tensor_copy(
                                cfm[:, fc * 512 + pc * 128:
                                    fc * 512 + (pc + 1) * 128],
                                self.PS[:, 6 * 512 + i * 128:
                                        6 * 512 + (i + 1) * 128]),
                                wait=[ttr4])
                        self.bset(6, tev)
                        cfm_t = tev
                tadd = k.op("vector", lambda E: E.tensor_add(
                    self.XG[:, :], self.XG[:, :], cfm), wait=[cfm_t, xgl])
                self.xg_free = k.dma("gpsimd", lambda E, t0=t0: E.dma_start(
                    dst[:, t0:t0 + TG].rearrange("(c p) t -> p c t", p=128),
                    self.XG[:, :].rearrange("p (c t) -> p c t", c=IC)),
                    wait=[tadd])
            self.ln_prev["tmp_free"] = prev_tr4
            self.ln_prev["sqf_free"] = prev_tr4
            self.ln_prev["txe_free"] = prev_tr4
            self.ln_prev["sqe_free"] = prev_tr4
        k.end_phase()

    def phase_dump(self, src):
        k = self.k
        k.begin_phase()
        for r in range(0, D, 128):
            k.dma("sync", lambda E, r=r: E.dma_start(
                self.out_d[r:r + 128, :], src[r:r + 128, :]))
        k.end_phase()

    def finish(self):
        k = self.k
        k.begin_phase()
        k.op("vector", lambda E: E.engine_nop())
        k.end_phase()

    def build(self):
        self.k.render()
        return self.nc



_CORR_ZEROS = np.zeros((NT, D), np.float32)


def make_in_maps(inputs, sh, prog):
    in_maps = []
    for c in range(8):
        m = dict(prep_core(inputs, c))
        m["colpack"] = sh["colpack"]
        m["rowpack"] = sh["rowpack"]
        m["maskpack"] = sh["maskpack"]
        m["idf32"] = sh["idf32"]
        m["idbf"] = sh["idbf"]
        m["corr0"] = _CORR_ZEROS
        m["corr1"] = _CORR_ZEROS
        for name in prog.win:
            m[name] = sh[name]
        in_maps.append(m)
    return in_maps


def kernel(**inputs):
    sh = prep_shared(inputs)
    p = Prog()
    p.colidx = sh["colidx"]
    p.rowidx = sh["rowidx"]
    p.maskidx = sh["maskidx"]

    p.phase_init()
    p.phase_wconv()
    p.phase_l0ffn(p.x_in, p.x_res)
    p.phase_l0attn(p.x_res, p.x_res)
    p.phase_l0ffn(p.x_res, p.x_res)
    p.phase_l1ffn(p.x_res, p.x_res, 0)
    p.phase_l1attn(p.x_res, p.x_res)
    p.phase_l1ffn(p.x_res, p.x_res, 1)
    p.phase_dump(p.x_res)
    p.finish()
    nc = p.build()

    res = run_bass_kernel_spmd(nc, make_in_maps(inputs, sh, p),
                               core_ids=list(range(8)))
    outs = []
    for c in range(8):
        o = np.asarray(res.results[c]["out"])
        outs.append(o.reshape(D, 2, SEQ).transpose(1, 2, 0))
    return np.ascontiguousarray(np.concatenate(outs, axis=0), dtype=np.float32)
